# version 03
# baseline (speedup 1.0000x reference)
import contextlib
import numpy as np
import concourse.bass as bass
import concourse.mybir as mybir
from concourse.bass_utils import run_bass_kernel_spmd

F32 = mybir.dt.float32
BF16 = mybir.dt.bfloat16
AF = mybir.ActivationFunctionType
ALU = mybir.AluOpType

D = 4096
SEQ = 8192
NCORES = 8
ROWS = SEQ // NCORES
NT = ROWS // 128
NJ = 8
OW = D // NJ
NC = D // 128
NSUB = 16
EPS = 1e-5

_CACHED = {}


def _build(nt=NT, dbg_no_transpose=False, dbg_w_on_sync=False, dbg_no_psum_read=False):
    rows = nt * 128
    nc = bass.Bass("TRN2", target_bir_lowering=False, detect_race_conditions=False)
    x_h = nc.dram_tensor("x", [rows, D], F32, kind="ExternalInput")
    wt_h = nc.dram_tensor("wt", [NJ * D, OW], F32, kind="ExternalInput")
    bias_h = nc.dram_tensor("bias", [1, D], F32, kind="ExternalInput")
    kk_h = nc.dram_tensor("kk", [1, 1], F32, kind="ExternalInput")
    aa_h = nc.dram_tensor("aa", [1, 1], F32, kind="ExternalInput")
    zscr_h = nc.dram_tensor("zscr", [rows, D], F32, kind="Internal")
    out_h = nc.dram_tensor("out", [rows, D], F32, kind="ExternalOutput")

    x = x_h.ap(); wt = wt_h.ap(); bias = bias_h.ap()
    zscr = zscr_h.ap(); out = out_h.ap()

    with contextlib.ExitStack() as es:
        ee = es.enter_context
        xA = ee(nc.sbuf_tensor("xA", [128, D], F32))
        xB = ee(nc.sbuf_tensor("xB", [128, D], F32))
        x1row = ee(nc.sbuf_tensor("x1row", [128, D], F32))
        h2row = ee(nc.sbuf_tensor("h2row", [128, D], BF16))
        h2T = ee(nc.sbuf_tensor("h2T", [128, NC * rows], BF16))
        wraw = ee(nc.sbuf_tensor("wraw", [128, 2 * 1024], F32))
        wbt = ee(nc.sbuf_tensor("wbt", [128, 2 * NC * OW], BF16))
        zcols = ee(nc.sbuf_tensor("zcols", [128, 2 * OW], F32))
        outfin = ee(nc.sbuf_tensor("outfin", [128, 2 * OW], F32))
        bias_oc = ee(nc.sbuf_tensor("bias_oc", [128, 2 * OW], F32))
        kkv = ee(nc.sbuf_tensor("kkv", [128, 1], F32))
        aav = ee(nc.sbuf_tensor("aav", [128, 1], F32))
        st6 = ee(nc.sbuf_tensor("st6", [128, 48], F32))
        mv = ee(nc.sbuf_tensor("mv", [128, 2 * nt], F32))
        tt = ee(nc.sbuf_tensor("tt", [128, nt], F32))
        uu = ee(nc.sbuf_tensor("uu", [128, nt], F32))
        t2 = ee(nc.sbuf_tensor("t2", [128, nt], F32))
        u2 = ee(nc.sbuf_tensor("u2", [128, nt], F32))
        rr = ee(nc.sbuf_tensor("rr", [128, 2 * nt], F32))
        sc = ee(nc.sbuf_tensor("sc", [128, 4 * nt], F32))
        tmp = ee(nc.sbuf_tensor("tmp", [128, nt], F32))
        pp0 = ee(nc.psum_tensor("pp0", [128, OW], F32))
        pp1 = ee(nc.psum_tensor("pp1", [128, OW], F32))
        s_bc = ee(nc.semaphore("s_bc"))
        s_x = [ee(nc.semaphore("s_x0")), ee(nc.semaphore("s_x1"))]
        s_stats = ee(nc.semaphore("s_stats"))
        s_sq = ee(nc.semaphore("s_sq"))
        s_act = ee(nc.semaphore("s_act"))
        s_h2T = ee(nc.semaphore("s_h2T"))
        s_w = [ee(nc.semaphore("s_w0")), ee(nc.semaphore("s_w1"))]
        s_tanh = ee(nc.semaphore("s_tanh"))
        s_zw = ee(nc.semaphore("s_zw"))
        s_zc = [ee(nc.semaphore("s_zc0")), ee(nc.semaphore("s_zc1"))]
        s_bo = [ee(nc.semaphore("s_bo0")), ee(nc.semaphore("s_bo1"))]
        s_mm = ee(nc.semaphore("s_mm"))
        s_ev = ee(nc.semaphore("s_ev"))
        s_out = [ee(nc.semaphore("s_out0")), ee(nc.semaphore("s_out1"))]
        block = ee(nc.Block())

        xbufs = [xA, xB]
        pps = [pp0, pp1]
        h2T3 = h2T.ap().rearrange("p (c s) -> p c s", c=NC, s=rows)

        @block.sync
        def _(sync):
            sync.dma_start(out=kkv[:, :], in_=kk_h.ap()[0:1, :].partition_broadcast(128)).then_inc(s_bc, 16)
            sync.dma_start(out=aav[:, :], in_=aa_h.ap()[0:1, :].partition_broadcast(128)).then_inc(s_bc, 16)
            for j in range(2):
                sync.dma_start(out=bias_oc[:, (j % 2) * OW:(j % 2 + 1) * OW],
                               in_=bias[0:1, j * OW:(j + 1) * OW].partition_broadcast(128)
                               ).then_inc(s_bo[j % 2], 16)
            for t in range(nt):
                if t >= 2:
                    sync.wait_ge(s_act, 2 * (t - 1))
                sync.dma_start(out=xbufs[t % 2][:, :], in_=x[t * 128:(t + 1) * 128, :]).then_inc(s_x[t % 2], 16)
            for j in range(2, NJ):
                sync.wait_ge(s_ev, nt * (j - 1))
                sync.dma_start(out=bias_oc[:, (j % 2) * OW:(j % 2 + 1) * OW],
                               in_=bias[0:1, j * OW:(j + 1) * OW].partition_broadcast(128)
                               ).then_inc(s_bo[j % 2], 16)

        def evac_group(vector, g):
            j, t = divmod(g, nt)
            zsl = zcols[:, (g % 2) * OW:(g % 2 + 1) * OW]
            osl = outfin[:, (g % 2) * OW:(g % 2 + 1) * OW]
            vector.wait_ge(s_zc[g % 2], 16 * (g // 2 + 1))
            vector.wait_ge(s_bo[j % 2], 16 * (j // 2 + 1))
            vector.tensor_add(zsl, zsl, bias_oc[:, (j % 2) * OW:(j % 2 + 1) * OW])
            vector.wait_ge(s_mm, g + 1)
            if g >= 2:
                vector.wait_ge(s_out[g % 2], 16 * (g // 2))
            vector.drain()
            if dbg_no_psum_read:
                vector.tensor_copy(osl, zsl).then_inc(s_ev, 1)
            else:
                vector.tensor_add(osl, pps[g % 2][:, :], zsl).then_inc(s_ev, 1)

        @block.vector
        def _(vector):
            vector.wait_ge(s_bc, 32)
            for t in range(nt):
                if t >= 2:
                    evac_group(vector, t - 2)
                vector.wait_ge(s_x[t % 2], 16 * (t // 2 + 1))
                xb = xbufs[t % 2]
                for k in range(8):
                    vector.bn_stats(st6[:, 6 * k:6 * k + 6], xb[:, 512 * k:512 * (k + 1)])
                vector.drain()
                vector.bn_aggr(mv[:, 2 * t:2 * t + 2], st6[:, :])
                vector.drain()
                vector.tensor_scalar_add(tt[:, t:t + 1], mv[:, 2 * t + 1:2 * t + 2], EPS
                                         ).then_inc(s_stats, 1)
                vector.wait_ge(s_sq, 2 * t + 1)
                r1 = rr[:, 2 * t:2 * t + 1]
                vector.reciprocal(r1, uu[:, t:t + 1])
                for _ in range(2):
                    vector.drain()
                    vector.tensor_mul(tmp[:, t:t + 1], tt[:, t:t + 1], r1)
                    vector.drain()
                    vector.tensor_mul(tmp[:, t:t + 1], tmp[:, t:t + 1], r1)
                    vector.drain()
                    vector.tensor_scalar(tmp[:, t:t + 1], tmp[:, t:t + 1], -0.5, 1.5, ALU.mult, ALU.add)
                    vector.drain()
                    vector.tensor_mul(r1, r1, tmp[:, t:t + 1])
                a1 = sc[:, 4 * t:4 * t + 1]; b1 = sc[:, 4 * t + 1:4 * t + 2]
                s2 = sc[:, 4 * t + 2:4 * t + 3]; b2 = sc[:, 4 * t + 3:4 * t + 4]
                vector.drain()
                vector.tensor_scalar(a1, r1, 2.0, 1.0, ALU.mult, ALU.add)
                vector.tensor_mul(b1, mv[:, 2 * t:2 * t + 1], r1)
                vector.drain()
                vector.tensor_scalar_mul(b1, b1, -2.0)
                vector.tensor_mul(tmp[:, t:t + 1], a1, a1)
                vector.drain()
                vector.tensor_mul(t2[:, t:t + 1], tmp[:, t:t + 1], mv[:, 2 * t + 1:2 * t + 2])
                vector.drain()
                vector.tensor_scalar_add(t2[:, t:t + 1], t2[:, t:t + 1], EPS
                                         ).then_inc(s_stats, 1)
                vector.wait_ge(s_sq, 2 * t + 2)
                r2 = rr[:, 2 * t + 1:2 * t + 2]
                vector.reciprocal(r2, u2[:, t:t + 1])
                for _ in range(2):
                    vector.drain()
                    vector.tensor_mul(tmp[:, t:t + 1], t2[:, t:t + 1], r2)
                    vector.drain()
                    vector.tensor_mul(tmp[:, t:t + 1], tmp[:, t:t + 1], r2)
                    vector.drain()
                    vector.tensor_scalar(tmp[:, t:t + 1], tmp[:, t:t + 1], -0.5, 1.5, ALU.mult, ALU.add)
                    vector.drain()
                    vector.tensor_mul(r2, r2, tmp[:, t:t + 1])
                vector.drain()
                vector.tensor_mul(s2, a1, r2)
                vector.drain()
                vector.tensor_mul(s2, s2, aav[:, :])
                vector.tensor_sub(b2, b1, mv[:, 2 * t:2 * t + 1])
                vector.drain()
                vector.tensor_mul(b2, b2, r2)
                vector.drain()
                vector.tensor_mul(b2, b2, aav[:, :]
                                  ).then_inc(s_stats, 1)
            for g in range(nt - 2, NJ * nt):
                evac_group(vector, g)

        def tanh_stripe(scalar, j):
            for sd in range(NSUB):
                idx = j * NSUB + sd
                if sd == 0 and j >= 2:
                    scalar.wait_ge(s_mm, nt * (j - 1))
                scalar.wait_ge(s_w[idx % 2], 16 * (idx // 2 + 1))
                scalar.activation(
                    wbt[:, (j % 2) * NC * OW + sd * 1024:(j % 2) * NC * OW + (sd + 1) * 1024],
                    wraw[:, (idx % 2) * 1024:(idx % 2) * 1024 + 1024],
                    AF.Tanh, scale=kkv[:, 0:1]).then_inc(s_tanh, 1)
                scalar.drain()
                nxt = idx + 2
                if nxt < NJ * NSUB:
                    jn, sn = divmod(nxt, NSUB)
                    sub = wt[jn * D + sn * 256:jn * D + (sn + 1) * 256, :]
                    scalar.dma_start(
                        out=wraw[:, (nxt % 2) * 1024:(nxt % 2) * 1024 + 1024].rearrange(
                            "p (c o) -> p c o", c=2),
                        in_=sub.rearrange("(c p) o -> p c o", p=128, c=2),
                    ).then_inc(s_w[nxt % 2], 16)

        def ln_tile(scalar, t):
            a1 = sc[:, 4 * t:4 * t + 1]; b1 = sc[:, 4 * t + 1:4 * t + 2]
            s2 = sc[:, 4 * t + 2:4 * t + 3]; b2 = sc[:, 4 * t + 3:4 * t + 4]
            xb = xbufs[t % 2]
            scalar.wait_ge(s_stats, 3 * t + 1)
            scalar.activation(uu[:, t:t + 1], tt[:, t:t + 1], AF.Sqrt).then_inc(s_sq, 1)
            scalar.wait_ge(s_stats, 3 * t + 2)
            scalar.activation(u2[:, t:t + 1], t2[:, t:t + 1], AF.Sqrt).then_inc(s_sq, 1)
            scalar.wait_ge(s_stats, 3 * t + 3)
            if t >= 1:
                scalar.wait_ge(s_h2T, 16 * t)
            scalar.activation(h2row[:, :], xb[:, :], AF.Identity, bias=b2[:, 0:1], scale=s2[:, 0:1]
                              ).then_inc(s_act, 1)
            scalar.drain()
            if dbg_no_transpose:
                scalar.dma_start(out=h2T[:, t * D:(t + 1) * D], in_=h2row[:, :]).then_inc(s_h2T, 16)
            else:
                scalar.dma_start_transpose(out=h2T3[:, :, t * 128:(t + 1) * 128],
                                           in_=h2row[:, :]).then_inc(s_h2T, 16)
            if t >= 1:
                scalar.wait_ge(s_zw, 16 * t)
            scalar.activation(x1row[:, :], xb[:, :], AF.Identity, bias=b1[:, 0:1], scale=a1[:, 0:1]
                              ).then_inc(s_act, 1)

        @block.scalar
        def _(scalar):
            for k in range(2):
                sub = wt[k * 256:(k + 1) * 256, :]
                scalar.dma_start(
                    out=wraw[:, (k % 2) * 1024:(k % 2) * 1024 + 1024].rearrange(
                        "p (c o) -> p c o", c=2),
                    in_=sub.rearrange("(c p) o -> p c o", p=128, c=2),
                ).then_inc(s_w[k % 2], 16)
            scalar.wait_ge(s_bc, 32)
            ln_tile(scalar, 0)
            if nt > 1:
                ln_tile(scalar, 1)
            tanh_stripe(scalar, 0)
            if nt > 2:
                ln_tile(scalar, 2)
            tanh_stripe(scalar, 1)
            for t in range(3, nt):
                ln_tile(scalar, t)
            for j in range(2, NJ):
                tanh_stripe(scalar, j)

        @block.tensor
        def _(tensor):
            for j in range(NJ):
                for t in range(nt):
                    g = j * nt + t
                    tensor.wait_ge(s_h2T, 16 * (t + 1))
                    tensor.wait_ge(s_tanh, NSUB * (j + 1))
                    if g >= 2:
                        tensor.wait_ge(s_ev, g - 1)
                    for c in range(NC):
                        mm = tensor.matmul(
                            pps[g % 2][:, :],
                            h2T[:, c * rows + t * 128:c * rows + (t + 1) * 128],
                            wbt[:, (j % 2) * NC * OW + c * OW:(j % 2) * NC * OW + (c + 1) * OW],
                            start=(c == 0), stop=(c == NC - 1))
                    mm.then_inc(s_mm, 1)

        @block.gpsimd
        def _(gpsimd):
            def store(g):
                j, t = divmod(g, nt)
                gpsimd.wait_ge(s_ev, g + 1)
                gpsimd.dma_start(out=out[t * 128:(t + 1) * 128, j * OW:(j + 1) * OW],
                                 in_=outfin[:, (g % 2) * OW:(g % 2 + 1) * OW]).then_inc(s_out[g % 2], 16)

            def zcload(g):
                j, t = divmod(g, nt)
                gpsimd.wait_ge(s_zw, 16 * (t + 1))
                if g >= 2:
                    gpsimd.wait_ge(s_ev, g - 1)
                gpsimd.dma_start(out=zcols[:, (g % 2) * OW:(g % 2 + 1) * OW],
                                 in_=zscr[t * 128:(t + 1) * 128, j * OW:(j + 1) * OW]
                                 ).then_inc(s_zc[g % 2], 16)
            for t in range(nt):
                gpsimd.wait_ge(s_act, 2 * (t + 1))
                gpsimd.dma_start(out=zscr[t * 128:(t + 1) * 128, :], in_=x1row[:, :]).then_inc(s_zw, 16)
                zcload(t)
                if t >= 2:
                    store(t - 2)
            for g in range(nt, NJ * nt):
                zcload(g)
                store(g - 2)
            store(NJ * nt - 2)
            store(NJ * nt - 1)
            gpsimd.wait_ge(s_out[0], 16 * (NJ * nt // 2))
            gpsimd.wait_ge(s_out[1], 16 * (NJ * nt // 2))

    return nc


def kernel(L0_attn_x, L0_ffn_weight, L0_ffn_bias, L0_ffn_kk, L0_ffn_aa):
    x = np.ascontiguousarray(np.asarray(L0_attn_x, dtype=np.float32).reshape(SEQ, D))
    W = np.asarray(L0_ffn_weight, dtype=np.float32)
    wt_strips = np.ascontiguousarray(W.T.reshape(D, NJ, OW).transpose(1, 0, 2)).reshape(NJ * D, OW)
    bias = np.ascontiguousarray(np.asarray(L0_ffn_bias, dtype=np.float32).reshape(1, D))
    kk = np.ascontiguousarray(np.asarray(L0_ffn_kk, dtype=np.float32).reshape(1, 1))
    aa = np.ascontiguousarray(np.asarray(L0_ffn_aa, dtype=np.float32).reshape(1, 1))

    if "nc" not in _CACHED:
        _CACHED["nc"] = _build()
    nc = _CACHED["nc"]
    in_maps = [{"x": np.ascontiguousarray(x[c * ROWS:(c + 1) * ROWS]),
                "wt": wt_strips, "bias": bias, "kk": kk, "aa": aa}
               for c in range(NCORES)]
    res = run_bass_kernel_spmd(nc, in_maps, core_ids=list(range(NCORES)))
    shards = [np.asarray(r["out"], dtype=np.float32) for r in res.results]
    return np.concatenate(shards, axis=0).reshape(1, SEQ, D)


# version 04
# speedup vs baseline: 1.1637x; 1.1637x over previous
import contextlib
import numpy as np
import concourse.bass as bass
import concourse.mybir as mybir
from concourse.bass_utils import run_bass_kernel_spmd

F32 = mybir.dt.float32
BF16 = mybir.dt.bfloat16
AF = mybir.ActivationFunctionType
ALU = mybir.AluOpType

D = 4096
SEQ = 8192
NCORES = 8
ROWS = SEQ // NCORES
NT = ROWS // 128
NJ = 8
OW = D // NJ
NC = D // 128
NSUB = 16
WRING = 4
EPS = 1e-5

_CACHED = {}


def _group_list(nt):
    return [(j, t) for j in range(NJ) for t in range(nt)]


def _build(nt=NT):
    rows = nt * 128
    nc = bass.Bass("TRN2", target_bir_lowering=False, detect_race_conditions=False)
    x_h = nc.dram_tensor("x", [rows, D], F32, kind="ExternalInput")
    wt_h = nc.dram_tensor("wt", [NJ * D, OW], F32, kind="ExternalInput")
    bias_h = nc.dram_tensor("bias", [1, D], F32, kind="ExternalInput")
    kk_h = nc.dram_tensor("kk", [1, 1], F32, kind="ExternalInput")
    aa_h = nc.dram_tensor("aa", [1, 1], F32, kind="ExternalInput")
    zscr_h = nc.dram_tensor("zscr", [rows, D], F32, kind="Internal")
    out_h = nc.dram_tensor("out", [rows, D], F32, kind="ExternalOutput")

    x = x_h.ap(); wt = wt_h.ap(); bias = bias_h.ap()
    zscr = zscr_h.ap(); out = out_h.ap()

    G = _group_list(nt)
    NG = len(G)
    end_pos = {}
    for p, (j, t) in enumerate(G):
        end_pos[j] = p
    NWSUB = NJ * NSUB

    with contextlib.ExitStack() as es:
        ee = es.enter_context
        xA = ee(nc.sbuf_tensor("xA", [128, D], F32))
        xB = ee(nc.sbuf_tensor("xB", [128, D], F32))
        x1row = ee(nc.sbuf_tensor("x1row", [128, D], F32))
        h2row = ee(nc.sbuf_tensor("h2row", [128, D // 2], BF16))
        h2T = ee(nc.sbuf_tensor("h2T", [128, NC * rows], BF16))
        wraw = ee(nc.sbuf_tensor("wraw", [128, WRING * 1024], F32))
        wbt = ee(nc.sbuf_tensor("wbt", [128, 2 * NC * OW], BF16))
        zcols = ee(nc.sbuf_tensor("zcols", [128, 2 * OW], F32))
        outfin = ee(nc.sbuf_tensor("outfin", [128, OW], F32))
        bias_oc = ee(nc.sbuf_tensor("bias_oc", [128, 2 * OW], F32))
        kkv = ee(nc.sbuf_tensor("kkv", [128, 1], F32))
        aav = ee(nc.sbuf_tensor("aav", [128, 1], F32))
        st6 = ee(nc.sbuf_tensor("st6", [128, 48], F32))
        mv = ee(nc.sbuf_tensor("mv", [128, 2 * nt], F32))
        tt = ee(nc.sbuf_tensor("tt", [128, nt], F32))
        t2 = ee(nc.sbuf_tensor("t2", [128, nt], F32))
        rr = ee(nc.sbuf_tensor("rr", [128, 4], F32))
        sc = ee(nc.sbuf_tensor("sc", [128, 5 * nt], F32))
        tq = ee(nc.sbuf_tensor("tq", [128, 2], F32))
        pp0 = ee(nc.psum_tensor("pp0", [128, OW], F32))
        pp1 = ee(nc.psum_tensor("pp1", [128, OW], F32))
        s_bc = ee(nc.semaphore("s_bc"))
        s_x = [ee(nc.semaphore("s_x0")), ee(nc.semaphore("s_x1"))]
        s_stats = ee(nc.semaphore("s_stats"))
        s_acth = ee(nc.semaphore("s_acth"))
        s_ax1 = ee(nc.semaphore("s_ax1"))
        s_h2T = ee(nc.semaphore("s_h2T"))
        s_w = [ee(nc.semaphore(f"s_w{i}")) for i in range(WRING)]
        s_tanh = ee(nc.semaphore("s_tanh"))
        s_zw = ee(nc.semaphore("s_zw"))
        s_zc = [ee(nc.semaphore("s_zc0")), ee(nc.semaphore("s_zc1"))]
        s_bo = [ee(nc.semaphore("s_bo0")), ee(nc.semaphore("s_bo1"))]
        s_mm = ee(nc.semaphore("s_mm"))
        s_ev = ee(nc.semaphore("s_ev"))
        s_out = ee(nc.semaphore("s_out"))
        block = ee(nc.Block())

        xbufs = [xA, xB]
        pps = [pp0, pp1]
        h2T3 = h2T.ap().rearrange("p (c s) -> p c s", c=NC, s=rows)

        @block.sync
        def _(sync):
            sync.dma_start(out=kkv[:, :], in_=kk_h.ap()[0:1, :].partition_broadcast(128)).then_inc(s_bc, 16)
            sync.dma_start(out=aav[:, :], in_=aa_h.ap()[0:1, :].partition_broadcast(128)).then_inc(s_bc, 16)
            for j in range(2):
                sync.dma_start(out=bias_oc[:, (j % 2) * OW:(j % 2 + 1) * OW],
                               in_=bias[0:1, j * OW:(j + 1) * OW].partition_broadcast(128)
                               ).then_inc(s_bo[j % 2], 16)
            for t in range(nt):
                if t >= 2:
                    sync.wait_ge(s_acth, 2 * (t - 1))
                    sync.wait_ge(s_ax1, t - 1)
                sync.dma_start(out=xbufs[t % 2][:, :], in_=x[t * 128:(t + 1) * 128, :]).then_inc(s_x[t % 2], 16)
            for j in range(2, NJ):
                sync.wait_ge(s_ev, end_pos[j - 2] + 1)
                sync.dma_start(out=bias_oc[:, (j % 2) * OW:(j % 2 + 1) * OW],
                               in_=bias[0:1, j * OW:(j + 1) * OW].partition_broadcast(128)
                               ).then_inc(s_bo[j % 2], 16)

        def newton(vector, r_slots, t_ap, n_iter):
            cur, alt = r_slots
            q = tq[:, 0:1]; w = tq[:, 1:2]
            for _ in range(n_iter):
                vector.drain()
                vector.tensor_mul(q, rr[:, cur:cur + 1], rr[:, cur:cur + 1])
                vector.drain()
                vector.tensor_scalar(w, q, t_ap, -0.5, ALU.mult, ALU.mult)
                vector.drain()
                vector.scalar_tensor_tensor(rr[:, alt:alt + 1], w, 1.5, rr[:, cur:cur + 1],
                                            ALU.add, ALU.mult)
                cur, alt = alt, cur
            return cur

        def evac_group(vector, p):
            j, t = G[p]
            zsl = zcols[:, (p % 2) * OW:(p % 2 + 1) * OW]
            vector.wait_ge(s_zc[p % 2], 16 * (p // 2 + 1))
            vector.wait_ge(s_bo[j % 2], 16 * ((j // 2) + 1))
            vector.tensor_add(zsl, zsl, bias_oc[:, (j % 2) * OW:(j % 2 + 1) * OW])
            vector.wait_ge(s_mm, p + 1)
            if p >= 1:
                vector.wait_ge(s_out, 16 * p)
            vector.drain()
            vector.tensor_add(outfin[:, :], pps[p % 2][:, :], zsl).then_inc(s_ev, 1)

        @block.vector
        def _(vector):
            vector.wait_ge(s_bc, 32)
            ev = 0

            def emit_evacs(upto):
                nonlocal ev
                while ev < upto:
                    evac_group(vector, ev)
                    ev += 1

            for t in range(nt):
                vector.wait_ge(s_x[t % 2], 16 * (t // 2 + 1))
                xb = xbufs[t % 2]
                for k in range(8):
                    vector.bn_stats(st6[:, 6 * k:6 * k + 6], xb[:, 512 * k:512 * (k + 1)])
                vector.drain()
                vector.bn_aggr(mv[:, 2 * t:2 * t + 2], st6[:, :])
                vector.drain()
                m_ap = mv[:, 2 * t:2 * t + 1]; v_ap = mv[:, 2 * t + 1:2 * t + 2]
                t_ap = tt[:, t:t + 1]
                vector.tensor_scalar_add(t_ap, v_ap, EPS)
                vector.drain()
                vector.reciprocal(rr[:, 0:1], t_ap)
                vector.drain()
                vector.tensor_scalar_min(rr[:, 0:1], rr[:, 0:1], 1.0)
                r1s = newton(vector, (0, 1), t_ap, 4)
                r1 = rr[:, r1s:r1s + 1]
                a1 = sc[:, 5 * t:5 * t + 1]; b1 = sc[:, 5 * t + 1:5 * t + 2]
                s2 = sc[:, 5 * t + 2:5 * t + 3]; b2 = sc[:, 5 * t + 3:5 * t + 4]
                dd = sc[:, 5 * t + 4:5 * t + 5]
                vector.drain()
                vector.tensor_scalar(a1, r1, 2.0, 1.0, ALU.mult, ALU.add)
                vector.scalar_tensor_tensor(b1, m_ap, -2.0, r1, ALU.mult, ALU.mult)
                vector.drain()
                vector.tensor_sub(dd, b1, m_ap)
                vector.tensor_mul(tq[:, 0:1], a1, a1)
                vector.drain()
                vector.tensor_scalar(t2[:, t:t + 1], tq[:, 0:1], v_ap, EPS, ALU.mult, ALU.add)
                vector.reciprocal(rr[:, 2:3], a1)
                vector.drain()
                vector.tensor_mul(rr[:, 2:3], rr[:, 2:3], r1)
                r2s = newton(vector, (2, 3), t2[:, t:t + 1], 2)
                r2 = rr[:, r2s:r2s + 1]
                vector.drain()
                vector.scalar_tensor_tensor(s2, a1, aav[:, 0:1], r2, ALU.mult, ALU.mult)
                vector.scalar_tensor_tensor(b2, dd, aav[:, 0:1], r2, ALU.mult, ALU.mult
                                            ).then_inc(s_stats, 1)
                emit_evacs(max(0, t - 2))
            emit_evacs(NG)

        wi = {"dma": WRING, "tanh": 0}

        def w_dma(scalar, k):
            jn, sn = divmod(k, NSUB)
            sub = wt[jn * D + sn * 256:jn * D + (sn + 1) * 256, :]
            scalar.dma_start(
                out=wraw[:, (k % WRING) * 1024:(k % WRING) * 1024 + 1024].rearrange(
                    "p (c o) -> p c o", c=2),
                in_=sub.rearrange("(c p) o -> p c o", p=128, c=2),
            ).then_inc(s_w[k % WRING], 16)

        def w_tanh(scalar, n):
            for _ in range(n):
                i = wi["tanh"]
                if i >= NWSUB:
                    return
                j, sd = divmod(i, NSUB)
                if sd == 0 and j >= 2:
                    scalar.wait_ge(s_mm, end_pos[j - 2] + 1)
                scalar.wait_ge(s_w[i % WRING], 16 * (i // WRING + 1))
                scalar.activation(
                    wbt[:, (j % 2) * NC * OW + sd * 1024:(j % 2) * NC * OW + (sd + 1) * 1024],
                    wraw[:, (i % WRING) * 1024:(i % WRING) * 1024 + 1024],
                    AF.Tanh, scale=kkv[:, 0:1]).then_inc(s_tanh, 1)
                k = wi["dma"]
                if k < NWSUB and k <= i + WRING - 2:
                    scalar.wait_ge(s_tanh, max(0, k - WRING + 1))
                    w_dma(scalar, k)
                    wi["dma"] += 1
                wi["tanh"] += 1

        @block.scalar
        def _(scalar):
            for k in range(WRING):
                w_dma(scalar, k)
            scalar.wait_ge(s_bc, 32)
            for t in range(nt):
                w_tanh(scalar, 2)
                a1 = sc[:, 5 * t:5 * t + 1]; b1 = sc[:, 5 * t + 1:5 * t + 2]
                s2 = sc[:, 5 * t + 2:5 * t + 3]; b2 = sc[:, 5 * t + 3:5 * t + 4]
                xb = xbufs[t % 2]
                scalar.wait_ge(s_stats, t + 1)
                scalar.wait_ge(s_h2T, 32 * t)
                scalar.activation(h2row[:, :], xb[:, 0:2048], AF.Identity,
                                  bias=b2[:, 0:1], scale=s2[:, 0:1]).then_inc(s_acth, 1)
                scalar.drain()
                scalar.dma_start_transpose(out=h2T3[:, 0:16, t * 128:(t + 1) * 128],
                                           in_=h2row[:, :]).then_inc(s_h2T, 16)
                w_tanh(scalar, 1)
                scalar.wait_ge(s_h2T, 32 * t + 16)
                scalar.activation(h2row[:, :], xb[:, 2048:4096], AF.Identity,
                                  bias=b2[:, 0:1], scale=s2[:, 0:1]).then_inc(s_acth, 1)
                scalar.drain()
                scalar.dma_start_transpose(out=h2T3[:, 16:32, t * 128:(t + 1) * 128],
                                           in_=h2row[:, :]).then_inc(s_h2T, 16)
                if t >= 1:
                    scalar.wait_ge(s_zw, 16 * t)
                scalar.activation(x1row[:, :], xb[:, :], AF.Identity,
                                  bias=b1[:, 0:1], scale=a1[:, 0:1]).then_inc(s_ax1, 1)
                w_tanh(scalar, 1)
            while wi["tanh"] < NWSUB:
                w_tanh(scalar, 1)

        @block.tensor
        def _(tensor):
            for p, (j, t) in enumerate(G):
                tensor.wait_ge(s_h2T, 32 * (t + 1))
                if p >= 2:
                    tensor.wait_ge(s_ev, p - 1)
                last_tw = -1
                for c in range(NC):
                    need = j * NSUB + c // 2 + 1
                    if need != last_tw and j <= 1:
                        tensor.wait_ge(s_tanh, need)
                        last_tw = need
                    elif c == 0 and j >= 2:
                        tensor.wait_ge(s_tanh, NSUB * (j + 1))
                    mm = tensor.matmul(
                        pps[p % 2][:, :],
                        h2T[:, c * rows + t * 128:c * rows + (t + 1) * 128],
                        wbt[:, (j % 2) * NC * OW + c * OW:(j % 2) * NC * OW + (c + 1) * OW],
                        start=(c == 0), stop=(c == NC - 1))
                mm.then_inc(s_mm, 1)

        @block.gpsimd
        def _(gpsimd):
            def zcload(p):
                j, t = G[p]
                gpsimd.wait_ge(s_zw, 16 * (t + 1))
                if p >= 2:
                    gpsimd.wait_ge(s_ev, p - 1)
                gpsimd.dma_start(out=zcols[:, (p % 2) * OW:(p % 2 + 1) * OW],
                                 in_=zscr[t * 128:(t + 1) * 128, j * OW:(j + 1) * OW]
                                 ).then_inc(s_zc[p % 2], 16)

            def store(p):
                j, t = G[p]
                gpsimd.wait_ge(s_ev, p + 1)
                gpsimd.dma_start(out=out[t * 128:(t + 1) * 128, j * OW:(j + 1) * OW],
                                 in_=outfin[:, :]).then_inc(s_out, 16)

            zi = 0
            si = 0

            def emit_z(upto):
                nonlocal zi
                while zi < min(upto, NG):
                    zcload(zi); zi += 1

            def emit_s(upto):
                nonlocal si
                while si < min(upto, NG):
                    store(si); si += 1

            for t in range(nt):
                gpsimd.wait_ge(s_ax1, t + 1)
                gpsimd.dma_start(out=zscr[t * 128:(t + 1) * 128, :], in_=x1row[:, :]).then_inc(s_zw, 16)
                emit_s(max(0, t - 2))
                emit_z(max(0, t - 1))
            for p in range(zi, NG):
                emit_s(p - 1)
                zcload(p)
            emit_s(NG)
            gpsimd.wait_ge(s_out, 16 * NG)

    return nc


def kernel(L0_attn_x, L0_ffn_weight, L0_ffn_bias, L0_ffn_kk, L0_ffn_aa):
    x = np.ascontiguousarray(np.asarray(L0_attn_x, dtype=np.float32).reshape(SEQ, D))
    W = np.asarray(L0_ffn_weight, dtype=np.float32)
    wt_strips = np.ascontiguousarray(W.T.reshape(D, NJ, OW).transpose(1, 0, 2)).reshape(NJ * D, OW)
    bias = np.ascontiguousarray(np.asarray(L0_ffn_bias, dtype=np.float32).reshape(1, D))
    kk = np.ascontiguousarray(np.asarray(L0_ffn_kk, dtype=np.float32).reshape(1, 1))
    aa = np.ascontiguousarray(np.asarray(L0_ffn_aa, dtype=np.float32).reshape(1, 1))

    if "nc" not in _CACHED:
        _CACHED["nc"] = _build()
    nc = _CACHED["nc"]
    in_maps = [{"x": np.ascontiguousarray(x[c * ROWS:(c + 1) * ROWS]),
                "wt": wt_strips, "bias": bias, "kk": kk, "aa": aa}
               for c in range(NCORES)]
    res = run_bass_kernel_spmd(nc, in_maps, core_ids=list(range(NCORES)))
    shards = [np.asarray(r["out"], dtype=np.float32) for r in res.results]
    return np.concatenate(shards, axis=0).reshape(1, SEQ, D)
